# version 17
# baseline (speedup 1.0000x reference)
import math
from contextlib import ExitStack

import numpy as np
import concourse.bass as bass
import concourse.mybir as mybir

F32 = mybir.dt.float32
BF16 = mybir.dt.bfloat16
AF = mybir.ActivationFunctionType
ALU = mybir.AluOpType
AX = mybir.AxisListType

CFG_REAL = dict(D=2048, I=1408, SI=2816, E=8, T=4096, TB=1024)
CFG_SIM = dict(D=256, I=256, SI=512, E=8, T=256, TB=128)

EPS = 1e-5
NEG = -1e30


class Plan:

    COMPUTE = ("pe", "dve", "act")
    DMAQ = {"qs": 6, "qg": 6}

    def __init__(self):
        self.engines = list(self.COMPUTE) + list(self.DMAQ)
        self.sem_names = list(self.COMPUTE) + [
            f"{q}{i}" for q, k in self.DMAQ.items() for i in range(k)]
        self.ops = {e: [] for e in self.engines}
        self.cnt = {s: 0 for s in self.sem_names}
        self.dma_idx = {q: 0 for q in self.DMAQ}
        self.writer = {}
        self.readers = {}
        self.last_wait = {e: {} for e in self.engines}
        self.pend_w = {e: [] for e in self.engines}
        self.pend_r = {e: [] for e in self.engines}

    def fence(self, eng, reads=(), writes=()):
        need = {}
        for r in reads:
            dep = self.writer.get(r)
            if dep is not None:
                need[dep[0]] = max(need.get(dep[0], 0), dep[1])
        for w in writes:
            dep = self.writer.get(w)
            if dep is not None:
                need[dep[0]] = max(need.get(dep[0], 0), dep[1])
            for s, v in self.readers.get(w, {}).items():
                need[s] = max(need.get(s, 0), v)
        items = self.ops[eng]
        for s in sorted(need):
            v = need[s]
            if self.last_wait[eng].get(s, 0) < v:
                items.append(("w", s, v))
                self.last_wait[eng][s] = v

    def emit(self, eng, fn, reads=(), writes=(), inc=True):
        need = {}

        def add(dep):
            if dep is not None:
                s, v = dep
                need[s] = max(need.get(s, 0), v)

        for r in reads:
            add(self.writer.get(r))
        for w in writes:
            add(self.writer.get(w))
            for s, v in self.readers.get(w, {}).items():
                need[s] = max(need.get(s, 0), v)
        if eng in self.DMAQ:
            assert inc
            slot = self.dma_idx[eng] % self.DMAQ[eng]
            self.dma_idx[eng] += 1
            sem = f"{eng}{slot}"
            step = 16
            need[sem] = max(need.get(sem, 0), self.cnt[sem])
        else:
            sem = eng
            step = 1
        items = self.ops[eng]
        for s in sorted(need):
            v = need[s]
            if self.last_wait[eng].get(s, 0) < v:
                items.append(("w", s, v))
                self.last_wait[eng][s] = v
        self.pend_w[eng].extend(writes)
        self.pend_r[eng].extend(reads)
        if inc:
            self.cnt[sem] += step
            c = self.cnt[sem]
            items.append(("i", fn, sem))
            for w in self.pend_w[eng]:
                self.writer[w] = (sem, c)
                self.readers[w] = {}
            for r in self.pend_r[eng]:
                self.readers.setdefault(r, {})[sem] = c
            self.pend_w[eng] = []
            self.pend_r[eng] = []
        else:
            items.append(("i", fn, None))


def build_program(cfg):
    D, I, SI, E, T, TB = (cfg[k] for k in ("D", "I", "SI", "E", "T", "TB"))
    KC, MI, MS = D // 128, I // 128, SI // 128
    NCH = T // TB
    NW = min(512, TB)
    HAL = [(h * NW, (h + 1) * NW) for h in range(TB // NW)]
    NH = len(HAL)
    NG = TB // 128
    SGRP = [list(range(s, min(s + MI, MS))) for s in range(0, MS, MI)]

    nc = bass.Bass("TRN2", target_bir_lowering=False, debug=False,
                   dynamic_dma_scratch_size=16384)

    dr = {}

    def dram(name, shape, dt=F32, kind="ExternalInput"):
        dr[name] = nc.dram_tensor(name, list(shape), dt, kind=kind)
        return dr[name]

    dram("xT", (D, T))
    dram("gw", (E, MI, 128, KC, 128))
    dram("uw", (E, MI, 128, KC, 128))
    dram("dwn", (E, KC, 128, MI, 128))
    dram("sgw", (MS, 128, KC, 128))
    dram("suw", (MS, 128, KC, 128))
    dram("sdw", (KC, 128, MS, 128))
    for nm in ("waq", "wak", "wav", "wag"):
        dram(nm, (KC, 128, KC, 128))
    dram("rwt", (128, KC, E))
    for nm in ("bq", "bk", "bv", "bg", "bsd"):
        dram(nm, (128, KC))
    dram("bsg", (128, MS))
    dram("bsu", (128, MS))
    dram("iden", (128, 128))
    dram("oneh", (E, E * 128), BF16)
    dram("yT", (D, T), kind="ExternalOutput")

    drb = {}
    for nm, shp in dict(
            gw=(E, MI, 128, KC, 128), uw=(E, MI, 128, KC, 128),
            dwn=(E, KC, 128, MI, 128), sgw=(MS, 128, KC, 128),
            suw=(MS, 128, KC, 128), sdw=(KC, 128, MS, 128),
            waq=(KC, 128, KC, 128), wak=(KC, 128, KC, 128),
            wav=(KC, 128, KC, 128), wag=(KC, 128, KC, 128)).items():
        drb[nm] = nc.dram_tensor(nm + "B", list(shp), BF16, kind="Internal")

    st = ExitStack()
    sb_bytes = [0]
    tiles = {}

    def sbuf(name, shape, dt):
        t = st.enter_context(nc.sbuf_tensor(name, list(shape), dt))
        sb_bytes[0] += shape[0] * shape[1] * (2 if dt == BF16 else 4)
        tiles[name] = t
        return t

    def psum(name, shape, dt=F32):
        t = st.enter_context(nc.psum_tensor(name, list(shape), dt))
        tiles[name] = t
        return t

    x_sb = [sbuf(f"x{k}", (128, TB), F32) for k in range(KC)]
    xn_sb = [sbuf(f"xn{k}", (128, TB), BF16) for k in range(KC)]
    hp_sb = [sbuf(f"hp{m}", (128, TB), BF16) for m in range(MI)]
    WG = [sbuf(f"WGb{p}", (128, KC * 128), BF16) for p in range(3)]
    WU = [sbuf(f"WUb{p}", (128, KC * 128), BF16) for p in range(3)]
    WD = [sbuf(f"WDb{p}", (128, MI * 128), BF16) for p in range(3)]
    WA = [WG[0], WG[1], WU[0], WU[1]]
    sg_sb = sbuf("sgt", (128, TB), BF16)
    tu_sb = sbuf("tut", (128, TB), BF16)
    bc_sb = sbuf("bct", (128, TB), F32)
    av = sbuf("av", (128, TB), F32)
    ag = bc_sb
    aS = sbuf("aS", (128, TB), F32)
    xc_sb = [sbuf(f"xc{p}", (128, TB), BF16) for p in range(2)]
    xq_sb = [sbuf("xq0", (128, TB), BF16)]
    dtmp = sbuf("dtmp", (128, TB), F32)
    aq = dtmp
    mean_r = sbuf("meanr", (1, TB), F32)
    r2_r = sbuf("r2r", (1, TB), F32)
    r3_r = sbuf("r3r", (1, TB), F32)
    rstd_r = sbuf("rstdr", (1, TB), F32)
    lgaug = sbuf("lgaug", (66, TB), F32)
    TC_sb = sbuf("TC", (E, TB), BF16)
    Caug = [sbuf(f"Caug{p}", (128, 66), F32) for p in range(1)]
    Ct = [sbuf(f"Ct{p}", (128, E), F32) for p in range(1)]
    l2t = [sbuf(f"l2t{p}", (128, E), F32) for p in range(1)]
    mk1 = [sbuf(f"mk1{p}", (128, E), F32) for p in range(1)]
    mk2 = [sbuf(f"mk2{p}", (128, E), F32) for p in range(1)]
    cmb = [sbuf(f"cmb{p}", (128, E), F32) for p in range(1)]
    cm2 = [sbuf(f"cm2{p}", (128, E), F32) for p in range(TB // 128)]
    m1t = [sbuf(f"m1t{p}", (128, 1), F32) for p in range(1)]
    m2t = [sbuf(f"m2t{p}", (128, 1), F32) for p in range(1)]
    dmt = [sbuf(f"dmt{p}", (128, 1), F32) for p in range(1)]
    w1t = [sbuf(f"w1t{p}", (128, 1), F32) for p in range(1)]
    w2t = [sbuf(f"w2t{p}", (128, 1), F32) for p in range(1)]
    bq_sb = sbuf("bqs", (128, KC), F32)
    bk_sb = sbuf("bks", (128, KC), F32)
    bv_sb = sbuf("bvs", (128, KC), F32)
    bg_sb = sbuf("bgs", (128, KC), F32)
    bsg_sb = sbuf("bsgs", (128, MS), F32)
    bsu_sb = sbuf("bsus", (128, MS), F32)
    bsd_sb = sbuf("bsds", (128, KC), F32)
    bsdh_sb = sbuf("bsdh", (128, KC), F32)
    rwt_sb = sbuf("rwts", (128, KC * E), BF16)
    iden_sb = sbuf("idens", (128, 128), F32)
    oneh_sb = sbuf("onehs", (E, E * 128), BF16)
    onesc = sbuf("onesc", (128, TB), BF16)
    csr_sb = sbuf("csr", (1, E), F32)
    ncsb_sb = sbuf("ncsb", (128, E), F32)
    onebf = sbuf("onebf", (128, 1), BF16)
    onef1 = sbuf("onef1", (1, 128), F32)
    carry = sbuf("carry", (128, KC), F32)

    NBANK = 8 if NW <= 512 else 4
    PSB = [psum(f"PS{i}", (128, NW), F32) for i in range(NBANK)]

    P = Plan()
    _pring = [0]

    def pgrab():
        i = _pring[0] % NBANK
        _pring[0] += 1
        return PSB[i], f"PS{i}"

    def pgrabs(n):
        return [pgrab() for _ in range(n)]

    def setup():
        for nm, t in (("bq", bq_sb), ("bk", bk_sb), ("bv", bv_sb),
                      ("bg", bg_sb), ("bsg", bsg_sb), ("bsu", bsu_sb),
                      ("bsd", bsd_sb), ("iden", iden_sb), ("oneh", oneh_sb)):
            src = dr[nm].ap()
            P.emit("qs", lambda E_, o=t, i=src: E_.dma_start(o[:], i),
                   writes=(t.name,))
        P.emit("qg", lambda E_: E_.dma_start(
            rwt_sb[:], dr["rwt"].ap().rearrange("p k e -> p (k e)")),
            writes=("rwts",))
        P.emit("dve", lambda E_: E_.memset(onesc[:], 1.0), writes=("onesc",))
        P.emit("dve", lambda E_: E_.memset(lgaug[:], 0.0), writes=("lgaug",))
        P.emit("dve", lambda E_: E_.memset(onebf[:], 1.0), writes=("onebf",))
        P.emit("dve", lambda E_: E_.memset(onef1[:], 1.0), writes=("onef1",))
        P.emit("dve", lambda E_: E_.memset(carry[:], 0.0),
               writes=tuple(f"carry{k}" for k in range(KC)))
        P.emit("dve", lambda E_: E_.tensor_scalar_mul(bsdh_sb[:], bsd_sb[:], 0.5),
               reads=("bsds",), writes=("bsdh",))
        cs_t, cs_n = pgrab()
        for k in range(KC):
            P.emit("pe", lambda E_, k=k, cs_t=cs_t: E_.matmul(
                cs_t[0:1, 0:E], onebf[:], rwt_sb[:, k * E:(k + 1) * E],
                start=(k == 0), stop=(k == KC - 1)),
                reads=("rwts", "onebf"), writes=(cs_n,), inc=(k == KC - 1))
        P.emit("dve", lambda E_, cs_t=cs_t: E_.tensor_scalar_mul(
            csr_sb[:], cs_t[0:1, 0:E], -1.0),
            reads=(cs_n,), writes=("csr",))
        nb_t, nb_n = pgrab()
        P.emit("pe", lambda E_, nb_t=nb_t: E_.matmul(
            nb_t[:, 0:E], onef1[:], csr_sb[:], start=True, stop=True),
            reads=("csr", "onef1"), writes=(nb_n,))
        P.emit("dve", lambda E_, nb_t=nb_t: E_.tensor_copy(ncsb_sb[:], nb_t[:, 0:E]),
               reads=(nb_n,), writes=("ncsb",))

    def emit_ln(with_router):
        sxa = pgrabs(NH)
        sxb = pgrabs(NH)
        raw = pgrabs(NH) if with_router else None
        for k in range(KC):
            p = k % 2
            P.emit("dve", lambda E_, k=k, p=p: E_.tensor_copy(xc_sb[p][:], x_sb[k][:]),
                   reads=(f"x{k}",), writes=(f"xc{p}",))
            P.emit("dve", lambda E_, p=p: E_.tensor_mul(xq_sb[0][:], xc_sb[p][:], xc_sb[p][:]),
                   reads=(f"xc{p}",), writes=("xq0",))
            for h, (a, b) in enumerate(HAL):
                P.emit("pe", lambda E_, p=p, a=a, b=b, k=k, t=sxa[h][0]: E_.matmul(
                    t[0:1, :], onebf[:], xc_sb[p][:, a:b],
                    start=(k == 0), stop=(k == KC - 1)),
                    reads=(f"xc{p}", "onebf"), writes=(sxa[h][1],), inc=False)
                if with_router:
                    P.emit("pe", lambda E_, p=p, a=a, b=b, k=k, t=raw[h][0]: E_.matmul(
                        t[0:E, :], rwt_sb[:, k * E:(k + 1) * E], xc_sb[p][:, a:b],
                        start=(k == 0), stop=(k == KC - 1)),
                        reads=(f"xc{p}", "rwts"), writes=(raw[h][1],), inc=False)
                P.emit("pe", lambda E_, p=p, a=a, b=b, k=k, t=sxb[h][0]: E_.matmul(
                    t[0:1, :], onebf[:], xq_sb[0][:, a:b],
                    start=(k == 0), stop=(k == KC - 1)),
                    reads=("xq0", "onebf"), writes=(sxb[h][1],),
                    inc=(h == NH - 1))
        for h, (a, b) in enumerate(HAL):
            P.emit("act", lambda E_, a=a, b=b, t=sxa[h][0]: E_.activation(
                mean_r[:, a:b], t[0:1, :], AF.Copy, scale=1.0 / D),
                reads=(sxa[h][1],), writes=("meanr",))
            P.emit("dve", lambda E_, a=a, b=b, t=sxb[h][0]: E_.tensor_scalar_mul(
                r2_r[:, a:b], t[0:1, :], 1.0 / D),
                reads=(sxb[h][1],), writes=("r2r",))
        P.emit("dve", lambda E_: E_.tensor_mul(r3_r[:], mean_r[:], mean_r[:]),
               reads=("meanr",), writes=("r3r",))
        P.emit("dve", lambda E_: E_.tensor_sub(r2_r[:], r2_r[:], r3_r[:]),
               reads=("r2r", "r3r"), writes=("r2r",))
        P.emit("dve", lambda E_: E_.tensor_scalar_add(r2_r[:], r2_r[:], EPS),
               reads=("r2r",), writes=("r2r",))
        P.emit("act", lambda E_: E_.activation(r3_r[:], r2_r[:], AF.Ln),
               reads=("r2r",), writes=("r3r",))
        P.emit("act", lambda E_: E_.activation(rstd_r[:], r3_r[:], AF.Exp, scale=-0.5),
               reads=("r3r",), writes=("rstdr",))
        mb = pgrabs(NH)
        rb = pgrabs(NH)
        for h, (a, b) in enumerate(HAL):
            P.emit("pe", lambda E_, a=a, b=b, t=mb[h][0]: E_.matmul(
                t[:], onef1[:], mean_r[:, a:b], start=True, stop=True),
                reads=("meanr", "onef1"), writes=(mb[h][1],))
            P.emit("pe", lambda E_, a=a, b=b, t=rb[h][0]: E_.matmul(
                t[:], onef1[:], rstd_r[:, a:b], start=True, stop=True),
                reads=("rstdr", "onef1"), writes=(rb[h][1],))
        if with_router:
            for h, (a, b) in enumerate(HAL):
                P.emit("dve", lambda E_, a=a, b=b, t=raw[h][0]: E_.tensor_copy(
                    lgaug[0:E, a:b], t[0:E, :]),
                    reads=(raw[h][1],), writes=("lgaug",))
            P.emit("dve", lambda E_: E_.tensor_copy(lgaug[32:33, :], mean_r[:]),
                   reads=("meanr",), writes=("lgaug",))
            P.emit("dve", lambda E_: E_.tensor_copy(lgaug[64:65, :], rstd_r[:]),
                   reads=("rstdr",), writes=("lgaug",))
        for k in range(KC):
            for h, (a, b) in enumerate(HAL):
                P.emit("dve", lambda E_, k=k, a=a, b=b, t=mb[h][0]: E_.tensor_sub(
                    dtmp[:, a:b], x_sb[k][:, a:b], t[:]),
                    reads=(f"x{k}", mb[h][1]), writes=("dtmp",))
                P.emit("dve", lambda E_, k=k, a=a, b=b, t=rb[h][0]: E_.tensor_mul(
                    xn_sb[k][:, a:b], dtmp[:, a:b], t[:]),
                    reads=("dtmp", rb[h][1]), writes=(f"xn{k}",))
        return raw

    def router_group_quanta(raw):
        for g in range(NG):
            def quantum(g=g):
                p = 0
                c0 = g * 128
                tr_t, tr_n = pgrab()
                P.emit("pe", lambda E_, c0=c0, t=tr_t: E_.transpose(
                    t[:, 0:66], lgaug[0:66, c0:c0 + 128], iden_sb[0:66, 0:66]),
                    reads=("lgaug", "idens"), writes=(tr_n,))
                P.emit("dve", lambda E_, p=p, t=tr_t: E_.tensor_copy(Caug[p][:], t[:, 0:66]),
                       reads=(tr_n,), writes=(f"Caug{p}",))
                P.emit("dve", lambda E_, p=p: E_.scalar_tensor_tensor(
                    Ct[p][:], ncsb_sb[:], Caug[p][:, 32:33], Caug[p][:, 0:E],
                    ALU.mult, ALU.add),
                    reads=("ncsb", f"Caug{p}"), writes=(f"Ct{p}",))
                P.emit("dve", lambda E_, p=p: E_.tensor_single_scalar(
                    Ct[p][:], Ct[p][:], Caug[p][:, 64:65], ALU.mult),
                    reads=(f"Ct{p}", f"Caug{p}"), writes=(f"Ct{p}",))
                P.emit("dve", lambda E_, p=p: E_.reduce_max(m1t[p][:], Ct[p][:], AX.X),
                       reads=(f"Ct{p}",), writes=(f"m1t{p}",))
                P.emit("dve", lambda E_, p=p: E_.tensor_single_scalar(
                    mk1[p][:], Ct[p][:], m1t[p][:], ALU.is_equal),
                    reads=(f"Ct{p}", f"m1t{p}"), writes=(f"mk1{p}",))
                P.emit("dve", lambda E_, p=p: E_.scalar_tensor_tensor(
                    l2t[p][:], mk1[p][:], NEG, Ct[p][:], ALU.mult, ALU.add),
                    reads=(f"mk1{p}", f"Ct{p}"), writes=(f"l2t{p}",))
                P.emit("dve", lambda E_, p=p: E_.reduce_max(m2t[p][:], l2t[p][:], AX.X),
                       reads=(f"l2t{p}",), writes=(f"m2t{p}",))
                P.emit("dve", lambda E_, p=p: E_.tensor_sub(dmt[p][:], m1t[p][:], m2t[p][:]),
                       reads=(f"m1t{p}", f"m2t{p}"), writes=(f"dmt{p}",))
                P.emit("act", lambda E_, p=p: E_.activation(w1t[p][:], dmt[p][:], AF.Sigmoid),
                       reads=(f"dmt{p}",), writes=(f"w1t{p}",))
                P.emit("act", lambda E_, p=p: E_.activation(w2t[p][:], dmt[p][:], AF.Sigmoid, scale=-1.0),
                       reads=(f"dmt{p}",), writes=(f"w2t{p}",))
                P.emit("dve", lambda E_, p=p: E_.tensor_single_scalar(
                    mk2[p][:], l2t[p][:], m2t[p][:], ALU.is_equal),
                    reads=(f"l2t{p}", f"m2t{p}"), writes=(f"mk2{p}",))
                P.emit("dve", lambda E_, p=p: E_.tensor_single_scalar(
                    cmb[p][:], mk1[p][:], w1t[p][:], ALU.mult),
                    reads=(f"mk1{p}", f"w1t{p}"), writes=(f"cmb{p}",))
                P.emit("dve", lambda E_, p=p, g=g: E_.scalar_tensor_tensor(
                    cm2[g][:], mk2[p][:], w2t[p][:], cmb[p][:], ALU.mult, ALU.add),
                    reads=(f"mk2{p}", f"w2t{p}", f"cmb{p}"), writes=(f"cm2{g}",))

            yield quantum
        for g in range(NG):
            def quantum2(g=g):
                c0 = g * 128
                tc_t, tc_n = pgrab()
                P.emit("pe", lambda E_, g=g, t=tc_t: E_.transpose(
                    t[0:E, 0:128], cm2[g][:], iden_sb[:]),
                    reads=(f"cm2{g}", "idens"), writes=(tc_n,))
                P.emit("dve", lambda E_, c0=c0, t=tc_t: E_.tensor_copy(
                    TC_sb[:, c0:c0 + 128], t[0:E, 0:128]),
                    reads=(tc_n,), writes=("TC",))

            yield quantum2
    wpar = dict(g=0, u=0, d=0, a=0)
    seen_w = set()
    rr = [0]

    def load_w(stream, buf_list, kchunks, base, skey, apf):
        p = wpar[stream] % len(buf_list)
        wpar[stream] += 1
        buf = buf_list[p]
        nm = buf.name
        n = kchunks * 128
        if skey not in seen_w:
            seen_w.add(skey)
            ap = apf(dr[base]).rearrange("p k m -> p (k m)")
            P.emit("qg", lambda E_, o=buf, i=ap, n=n:
                   E_.dma_start(o[:, 0:n], i), writes=(nm,))
            sap = apf(drb[base]).rearrange("p k m -> p (k m)")
            P.emit("qs", lambda E_, o=sap, i=buf, n=n:
                   E_.dma_start(o, i[:, 0:n]),
                   reads=(nm,), writes=("scr:" + skey,))
        else:
            q = ("qs", "qg")[rr[0] % 2]
            rr[0] += 1
            sap = apf(drb[base]).rearrange("p k m -> p (k m)")
            P.emit(q, lambda E_, o=buf, i=sap, n=n:
                   E_.dma_start(o[:, 0:n], i),
                   reads=("scr:" + skey,), writes=(nm,))
        return buf, nm

    def mm_block(pt, pn, a, b, wbuf, wnm, rhs_tiles, rhs_names):
        n = len(rhs_tiles)
        for k in range(n):
            P.emit("pe", lambda E_, k=k: E_.matmul(
                pt[:], wbuf[:, k * 128:(k + 1) * 128], rhs_tiles[k][:, a:b],
                start=(k == 0), stop=(k == n - 1)),
                reads=(wnm, rhs_names[k]),
                writes=(pn,), inc=(k == n - 1))

    def emit_expert_gu(e, rq=None):
        xn_names = [f"xn{k}" for k in range(KC)]
        for m in range(MI):
            if rq is not None:
                q = next(rq, None)
                if q is not None:
                    q()
            gb, gnm = load_w("g", WG, KC, "gw", f"gw{e}_{m}",
                             lambda h, e=e, m=m: h.ap()[e, m])
            ub, unm = load_w("u", WU, KC, "uw", f"uw{e}_{m}",
                             lambda h, e=e, m=m: h.ap()[e, m])
            Pg = pgrabs(NH)
            Pu = pgrabs(NH)
            P.fence("pe", reads=(gnm, unm),
                    writes=tuple(n for _, n in Pg + Pu))
            for h, (a, b) in enumerate(HAL):
                mm_block(Pg[h][0], Pg[h][1], a, b, gb, gnm, xn_sb, xn_names)
                mm_block(Pu[h][0], Pu[h][1], a, b, ub, unm, xn_sb, xn_names)
                P.emit("act", lambda E_, a=a, b=b, t=Pg[h][0]: E_.activation(
                    sg_sb[:, a:b], t[:], AF.Silu),
                    reads=(Pg[h][1],), writes=(f"sg{h}",))
                P.emit("dve", lambda E_, a=a, b=b, t=Pu[h][0]: E_.tensor_copy(
                    tu_sb[:, a:b], t[:]),
                    reads=(Pu[h][1],), writes=(f"tu{h}",))
                P.emit("dve", lambda E_, a=a, b=b, m=m: E_.tensor_mul(
                    hp_sb[m][:, a:b], sg_sb[:, a:b], tu_sb[:, a:b]),
                    reads=(f"sg{h}", f"tu{h}"), writes=(f"hp{m}_{h}",))

    def emit_expert_down(e, scale):
        bcb = pgrabs(NH)
        for h, (a, b) in enumerate(HAL):
            P.emit("pe", lambda E_, a=a, b=b, e=e, t=bcb[h][0]: E_.matmul(
                t[:], oneh_sb[:, e * 128:(e + 1) * 128], TC_sb[:, a:b],
                start=True, stop=True),
                reads=("TC", "onehs"), writes=(bcb[h][1],))
            P.emit("dve", lambda E_, a=a, b=b, t=bcb[h][0]: E_.tensor_copy(
                bc_sb[:, a:b], t[:]),
                reads=(bcb[h][1],), writes=("bct",))
        for j in range(KC):
            db, dnm = load_w("d", WD, MI, "dwn", f"dw{e}_{j}",
                             lambda h, e=e, j=j: h.ap()[e, j])
            Pd = pgrabs(NH)
            P.fence("pe", reads=(dnm,), writes=tuple(n for _, n in Pd))
            for h, (a, b) in enumerate(HAL):
                mm_block(Pd[h][0], Pd[h][1], a, b, db, dnm, hp_sb,
                         [f"hp{k}_{h}" for k in range(MI)])
                P.emit("dve", lambda E_, a=a, b=b, t=Pd[h][0]: E_.tensor_mul(
                    dtmp[:, a:b], t[:], bc_sb[:, a:b]),
                    reads=(Pd[h][1], "bct"), writes=("dtmp",))
                P.emit("dve", lambda E_, a=a, b=b, j=j, s=scale:
                       E_.scalar_tensor_tensor(
                           x_sb[j][:, a:b], dtmp[:, a:b], s, x_sb[j][:, a:b],
                           ALU.mult, ALU.add),
                       reads=("dtmp", f"x{j}"), writes=(f"x{j}",))

    def emit_shared(scale):
        xn_names = [f"xn{k}" for k in range(KC)]
        for gi, grp in enumerate(SGRP):
            for mi, m in enumerate(grp):
                gb, gnm = load_w("g", WG, KC, "sgw", f"sg{m}",
                                 lambda h, m=m: h.ap()[m])
                ub, unm = load_w("u", WU, KC, "suw", f"su{m}",
                                 lambda h, m=m: h.ap()[m])
                Pg = pgrabs(NH)
                Pu = pgrabs(NH)
                P.fence("pe", reads=(gnm, unm),
                        writes=tuple(n for _, n in Pg + Pu))
                for h, (a, b) in enumerate(HAL):
                    mm_block(Pg[h][0], Pg[h][1], a, b, gb, gnm, xn_sb, xn_names)
                    mm_block(Pu[h][0], Pu[h][1], a, b, ub, unm, xn_sb, xn_names)
                    P.emit("act", lambda E_, a=a, b=b, t=Pg[h][0], m=m: E_.activation(
                        sg_sb[:, a:b], t[:], AF.Silu,
                        bias=bsg_sb[:, m:m + 1]),
                        reads=(Pg[h][1], "bsgs"), writes=(f"sg{h}",))
                    P.emit("dve", lambda E_, a=a, b=b, t=Pu[h][0], m=m:
                           E_.tensor_single_scalar(
                               tu_sb[:, a:b], t[:], bsu_sb[:, m:m + 1],
                               ALU.add),
                           reads=(Pu[h][1], "bsus"), writes=(f"tu{h}",))
                    P.emit("dve", lambda E_, a=a, b=b, mi=mi: E_.tensor_mul(
                        hp_sb[mi][:, a:b], sg_sb[:, a:b], tu_sb[:, a:b]),
                        reads=(f"sg{h}", f"tu{h}"), writes=(f"hp{mi}_{h}",))
            lastg = gi == len(SGRP) - 1
            for j in range(KC):
                db, dnm = load_w(
                    "d", WD, len(grp), "sdw", f"sd{j}_{gi}",
                    lambda h, j=j, g0=grp[0], gl=len(grp):
                        h.ap()[j][:, g0:g0 + gl, :])
                Pd = pgrabs(NH)
                P.fence("pe", reads=(dnm,), writes=tuple(n for _, n in Pd))
                for h, (a, b) in enumerate(HAL):
                    mm_block(Pd[h][0], Pd[h][1], a, b, db, dnm, hp_sb,
                             [f"hp{k}_{h}" for k in range(len(grp))])
                    if lastg:
                        P.emit("dve", lambda E_, a=a, b=b, j=j, t=Pd[h][0], s=scale:
                               E_.tensor_scalar(dtmp[:, a:b], t[:], s,
                                                bsdh_sb[:, j:j + 1],
                                                ALU.mult, ALU.add),
                               reads=(Pd[h][1], "bsdh"), writes=("dtmp",))
                        P.emit("dve", lambda E_, a=a, b=b, j=j: E_.tensor_add(
                            x_sb[j][:, a:b], x_sb[j][:, a:b], dtmp[:, a:b]),
                            reads=("dtmp", f"x{j}"), writes=(f"x{j}",))
                    else:
                        P.emit("dve", lambda E_, a=a, b=b, j=j, t=Pd[h][0], s=scale:
                               E_.scalar_tensor_tensor(
                                   x_sb[j][:, a:b], t[:], s,
                                   x_sb[j][:, a:b], ALU.mult, ALU.add),
                               reads=(Pd[h][1], f"x{j}"), writes=(f"x{j}",))

    def emit_attn():
        xn_names = [f"xn{k}" for k in range(KC)]
        for ec in range(KC):
            pro = {}
            for nm in ("waq", "wak", "wav", "wag"):
                wb, wnm = load_w("a", WA, KC, nm, f"{nm}{ec}",
                                 lambda h, ec=ec: h.ap()[ec])
                pro[nm] = pgrabs(NH)
                P.fence("pe", reads=(wnm,),
                        writes=tuple(n for _, n in pro[nm]))
                for h, (a, b) in enumerate(HAL):
                    mm_block(pro[nm][h][0], pro[nm][h][1], a, b, wb, wnm,
                             xn_sb, xn_names)
            for h, (a, b) in enumerate(HAL):
                P.emit("act", lambda E_, a=a, b=b, ec=ec, t=pro["wag"][h][0]:
                       E_.activation(ag[:, a:b], t[:], AF.Sigmoid,
                                     bias=bg_sb[:, ec:ec + 1]),
                       reads=(pro["wag"][h][1], "bgs"), writes=("bct",))
                P.emit("dve", lambda E_, a=a, b=b, ec=ec, t=pro["wav"][h][0]:
                       E_.tensor_single_scalar(
                           av[:, a:b], t[:], bv_sb[:, ec:ec + 1], ALU.add),
                       reads=(pro["wav"][h][1], "bvs"), writes=("av",))
                P.emit("dve", lambda E_, a=a, b=b, ec=ec, t=pro["wak"][h][0]:
                       E_.scalar_tensor_tensor(
                           av[:, a:b], t[:], bk_sb[:, ec:ec + 1], av[:, a:b],
                           ALU.add, ALU.mult),
                       reads=(pro["wak"][h][1], "bks", "av"), writes=("av",))
                P.emit("dve", lambda E_, a=a, b=b, ec=ec, t=pro["waq"][h][0]:
                       E_.scalar_tensor_tensor(
                           aq[:, a:b], t[:], bq_sb[:, ec:ec + 1], ag[:, a:b],
                           ALU.add, ALU.mult),
                       reads=(pro["waq"][h][1], "bqs", "bct"), writes=("dtmp",))
            P.emit("dve", lambda E_, ec=ec: E_.tensor_tensor_scan(
                aS[:], onesc[:], av[:], carry[:, ec:ec + 1], ALU.mult, ALU.add),
                reads=("onesc", "av", f"carry{ec}"), writes=("aS",))
            P.emit("dve", lambda E_, ec=ec: E_.tensor_copy(
                carry[:, ec:ec + 1], aS[:, TB - 1:TB]),
                reads=("aS",), writes=(f"carry{ec}",))
            P.emit("dve", lambda E_: E_.tensor_mul(ag[:], aq[:], aS[:]),
                   reads=("dtmp", "aS"), writes=("bct",))
            P.emit("dve", lambda E_, ec=ec: E_.tensor_add(
                x_sb[ec][:], x_sb[ec][:], ag[:]),
                reads=("bct", f"x{ec}"), writes=(f"x{ec}",))

    setup()
    for c in range(NCH):
        c0 = c * TB
        for k in range(KC):
            src = dr["xT"].ap()[k * 128:(k + 1) * 128, c0:c0 + TB]
            P.emit("qs", lambda E_, k=k, src=src: E_.dma_start(x_sb[k][:], src),
                   writes=(f"x{k}",))
        for app in range(2):
            raw = emit_ln(with_router=True)
            rq = router_group_quanta(raw)
            emit_expert_gu(0, rq)
            for q in rq:
                q()
            emit_expert_down(0, 0.5)
            for e in range(1, E):
                emit_expert_gu(e)
                emit_expert_down(e, 0.5)
            emit_shared(0.5)
            if app == 0:
                emit_ln(with_router=False)
                emit_attn()
        for k in range(KC):
            dst = dr["yT"].ap()[k * 128:(k + 1) * 128, c0:c0 + TB]
            P.emit("qs", lambda E_, k=k, dst=dst: E_.dma_start(dst, x_sb[k][:]),
                   reads=(f"x{k}",))

    sems = {n: st.enter_context(nc.semaphore(f"sem_{n}")) for n in P.sem_names}
    blk = st.enter_context(nc.Block())

    def body(items, final_waits=()):
        def run(E_):
            for it in items:
                if it[0] == "w":
                    E_.wait_ge(sems[it[1]], it[2])
                else:
                    inst = it[1](E_)
                    if it[2] is not None:
                        inst.then_inc(sems[it[2]], 16 if it[2][0] == "q" else 1)
            for s, v in final_waits:
                E_.wait_ge(sems[s], v)
        return run

    qs_final = [(s, P.cnt[s]) for s in P.sem_names
                if s.startswith("qs") and P.cnt[s] > 0]
    blk.tensor(body(P.ops["pe"]))
    blk.vector(body(P.ops["dve"]))
    blk.scalar(body(P.ops["act"]))
    blk.sync(body(P.ops["qs"], qs_final))
    blk.gpsimd(body(P.ops["qg"]))
    st.close()

    n_inst = {k: len([i for i in v if i[0] == "i"]) for k, v in P.ops.items()}
    n_wait = {k: len([i for i in v if i[0] == "w"]) for k, v in P.ops.items()}
    info = dict(sbuf_mb=sb_bytes[0] / 2**20, n_inst=n_inst, n_wait=n_wait)
    return nc, info


def blockA(W, KCd, MId):
    return np.ascontiguousarray(
        np.asarray(W, dtype=np.float32).reshape(KCd, 128, MId, 128).transpose(2, 1, 0, 3))


def bias_cols(b, n):
    return np.ascontiguousarray(np.asarray(b, dtype=np.float32).reshape(n, 128).T)


def prep_weights(inp, cfg, pfx="L0"):
    D, I, SI, E = cfg["D"], cfg["I"], cfg["SI"], cfg["E"]
    KC, MI, MS = D // 128, I // 128, SI // 128
    g = lambda k: np.asarray(inp[f"{pfx}_{k}"], dtype=np.float32)
    w = {}
    w["gw"] = np.stack([blockA(g("ffn_gate_w")[e], KC, MI) for e in range(E)])
    w["uw"] = np.stack([blockA(g("ffn_up_w")[e], KC, MI) for e in range(E)])
    w["dwn"] = np.stack([blockA(g("ffn_down_w")[e], MI, KC) for e in range(E)])
    w["sgw"] = blockA(g("ffn_s_gate_w"), KC, MS)
    w["suw"] = blockA(g("ffn_s_up_w"), KC, MS)
    w["sdw"] = blockA(g("ffn_s_down_w"), MS, KC)
    w["waq"] = blockA(g("attn_Wq").T, KC, KC)
    w["wak"] = blockA(g("attn_Wk").T, KC, KC)
    w["wav"] = blockA(g("attn_Wv").T, KC, KC)
    w["wag"] = blockA(g("attn_Wg").T, KC, KC)
    w["rwt"] = np.ascontiguousarray(
        g("ffn_router_w").T.reshape(KC, 128, E).transpose(1, 0, 2))
    w["bq"] = bias_cols(g("attn_bq"), KC)
    w["bk"] = bias_cols(g("attn_bk"), KC)
    w["bv"] = bias_cols(g("attn_bv"), KC)
    w["bg"] = bias_cols(g("attn_bg"), KC)
    w["bsg"] = bias_cols(g("ffn_s_gate_b"), MS)
    w["bsu"] = bias_cols(g("ffn_s_up_b"), MS)
    w["bsd"] = bias_cols(g("ffn_s_down_b"), KC)
    w["iden"] = np.eye(128, dtype=np.float32)
    import ml_dtypes
    oneh = np.zeros((E, E * 128), dtype=ml_dtypes.bfloat16)
    for e in range(E):
        oneh[e, e * 128:(e + 1) * 128] = 1.0
    w["oneh"] = oneh
    return w


_CACHE = {}


def kernel(**inputs):
    from concourse.bass_utils import run_bass_kernel_spmd
    cfg = CFG_REAL
    if "nc" not in _CACHE:
        _CACHE["nc"], _CACHE["info"] = build_program(cfg)
    nc = _CACHE["nc"]
    w = prep_weights(inputs, cfg)
    x = np.asarray(inputs["L0_attn_x"], dtype=np.float32)
    n = x.shape[0]
    in_maps = []
    for c in range(n):
        m = dict(w)
        m["xT"] = np.ascontiguousarray(x[c].T)
        in_maps.append(m)
    res = run_bass_kernel_spmd(nc, in_maps, core_ids=list(range(n)))
    out = np.stack([np.asarray(r["yT"], dtype=np.float32).T for r in res.results])
    return np.ascontiguousarray(out)


# version 18 (final)
# speedup vs baseline: 1.1455x; 1.1455x over previous
import math
from contextlib import ExitStack

import numpy as np
import concourse.bass as bass
import concourse.mybir as mybir

F32 = mybir.dt.float32
BF16 = mybir.dt.bfloat16
AF = mybir.ActivationFunctionType
ALU = mybir.AluOpType
AX = mybir.AxisListType

CFG_REAL = dict(D=2048, I=1408, SI=2816, E=8, T=4096, TB=1024)
CFG_SIM = dict(D=256, I=256, SI=512, E=8, T=256, TB=128)

EPS = 1e-5
NEG = -1e30


class Plan:

    COMPUTE = ("pe", "dve", "act")
    DMAQ = {"qs": 6, "qg": 6}

    def __init__(self):
        self.engines = list(self.COMPUTE) + list(self.DMAQ)
        self.sem_names = list(self.COMPUTE) + [
            f"{q}{i}" for q, k in self.DMAQ.items() for i in range(k)]
        self.ops = {e: [] for e in self.engines}
        self.cnt = {s: 0 for s in self.sem_names}
        self.dma_idx = {q: 0 for q in self.DMAQ}
        self.writer = {}
        self.readers = {}
        self.last_wait = {e: {} for e in self.engines}
        self.pend_w = {e: [] for e in self.engines}
        self.pend_r = {e: [] for e in self.engines}

    def fence(self, eng, reads=(), writes=()):
        need = {}
        for r in reads:
            dep = self.writer.get(r)
            if dep is not None:
                need[dep[0]] = max(need.get(dep[0], 0), dep[1])
        for w in writes:
            dep = self.writer.get(w)
            if dep is not None:
                need[dep[0]] = max(need.get(dep[0], 0), dep[1])
            for s, v in self.readers.get(w, {}).items():
                need[s] = max(need.get(s, 0), v)
        items = self.ops[eng]
        for s in sorted(need):
            v = need[s]
            if self.last_wait[eng].get(s, 0) < v:
                items.append(("w", s, v))
                self.last_wait[eng][s] = v

    def emit(self, eng, fn, reads=(), writes=(), inc=True):
        need = {}

        def add(dep):
            if dep is not None:
                s, v = dep
                need[s] = max(need.get(s, 0), v)

        for r in reads:
            add(self.writer.get(r))
        for w in writes:
            add(self.writer.get(w))
            for s, v in self.readers.get(w, {}).items():
                need[s] = max(need.get(s, 0), v)
        if eng in self.DMAQ:
            assert inc
            slot = self.dma_idx[eng] % self.DMAQ[eng]
            self.dma_idx[eng] += 1
            sem = f"{eng}{slot}"
            step = 16
            need[sem] = max(need.get(sem, 0), self.cnt[sem])
        else:
            sem = eng
            step = 1
        items = self.ops[eng]
        for s in sorted(need):
            v = need[s]
            if self.last_wait[eng].get(s, 0) < v:
                items.append(("w", s, v))
                self.last_wait[eng][s] = v
        self.pend_w[eng].extend(writes)
        self.pend_r[eng].extend(reads)
        if inc:
            self.cnt[sem] += step
            c = self.cnt[sem]
            items.append(("i", fn, sem))
            for w in self.pend_w[eng]:
                self.writer[w] = (sem, c)
                self.readers[w] = {}
            for r in self.pend_r[eng]:
                self.readers.setdefault(r, {})[sem] = c
            self.pend_w[eng] = []
            self.pend_r[eng] = []
        else:
            items.append(("i", fn, None))


def build_program(cfg):
    D, I, SI, E, T, TB = (cfg[k] for k in ("D", "I", "SI", "E", "T", "TB"))
    KC, MI, MS = D // 128, I // 128, SI // 128
    NCH = T // TB
    NW = min(512, TB)
    HAL = [(h * NW, (h + 1) * NW) for h in range(TB // NW)]
    NH = len(HAL)
    NG = TB // 128
    SGRP = [list(range(s, min(s + MI, MS))) for s in range(0, MS, MI)]

    nc = bass.Bass("TRN2", target_bir_lowering=False, debug=False,
                   dynamic_dma_scratch_size=16384)

    dr = {}

    def dram(name, shape, dt=F32, kind="ExternalInput"):
        dr[name] = nc.dram_tensor(name, list(shape), dt, kind=kind)
        return dr[name]

    dram("xT", (D, T))
    dram("gw", (E, MI, 128, KC, 128))
    dram("uw", (E, MI, 128, KC, 128))
    dram("dwn", (E, KC, 128, MI, 128))
    dram("sgw", (MS, 128, KC, 128))
    dram("suw", (MS, 128, KC, 128))
    dram("sdw", (KC, 128, MS, 128))
    for nm in ("waq", "wak", "wav", "wag"):
        dram(nm, (KC, 128, KC, 128))
    dram("rwt", (128, KC, E))
    for nm in ("bq", "bk", "bv", "bg", "bsd"):
        dram(nm, (128, KC))
    dram("bsg", (128, MS))
    dram("bsu", (128, MS))
    dram("iden", (128, 128))
    dram("oneh", (E, E * 128), BF16)
    dram("yT", (D, T), kind="ExternalOutput")

    drb = {}
    for nm, shp in dict(
            gw=(E, MI, 128, KC, 128), uw=(E, MI, 128, KC, 128),
            dwn=(E, KC, 128, MI, 128), sgw=(MS, 128, KC, 128),
            suw=(MS, 128, KC, 128), sdw=(KC, 128, MS, 128),
            waq=(KC, 128, KC, 128), wak=(KC, 128, KC, 128),
            wav=(KC, 128, KC, 128), wag=(KC, 128, KC, 128)).items():
        drb[nm] = nc.dram_tensor(nm + "B", list(shp), BF16, kind="Internal")

    st = ExitStack()
    sb_bytes = [0]
    tiles = {}

    def sbuf(name, shape, dt):
        t = st.enter_context(nc.sbuf_tensor(name, list(shape), dt))
        sb_bytes[0] += shape[0] * shape[1] * (2 if dt == BF16 else 4)
        tiles[name] = t
        return t

    def psum(name, shape, dt=F32):
        t = st.enter_context(nc.psum_tensor(name, list(shape), dt))
        tiles[name] = t
        return t

    x_sb = [sbuf(f"x{k}", (128, TB), F32) for k in range(KC)]
    xn_sb = [sbuf(f"xn{k}", (128, TB), BF16) for k in range(KC)]
    hp_sb = [sbuf(f"hp{m}", (128, TB), BF16) for m in range(MI)]
    WG = [sbuf(f"WGb{p}", (128, KC * 128), BF16) for p in range(3)]
    WU = [sbuf(f"WUb{p}", (128, KC * 128), BF16) for p in range(3)]
    WD = [sbuf(f"WDb{p}", (128, MI * 128), BF16) for p in range(3)]
    WA = [WG[0], WG[1], WU[0], WU[1]]
    sg_sb = sbuf("sgt", (128, TB), BF16)
    tu_sb = sbuf("tut", (128, TB), BF16)
    bc_sb = sbuf("bct", (128, TB), F32)
    av = sbuf("av", (128, TB), F32)
    ag = bc_sb
    aS = sbuf("aS", (128, TB), F32)
    xc_sb = [sbuf(f"xc{p}", (128, TB), BF16) for p in range(2)]
    xq_sb = [sbuf("xq0", (128, TB), BF16)]
    dtmp = sbuf("dtmp", (128, TB), F32)
    aq = dtmp
    mean_r = sbuf("meanr", (1, TB), F32)
    r2_r = sbuf("r2r", (1, TB), F32)
    r3_r = sbuf("r3r", (1, TB), F32)
    rstd_r = sbuf("rstdr", (1, TB), F32)
    lgaug = sbuf("lgaug", (66, TB), F32)
    TC_sb = sbuf("TC", (E, TB), BF16)
    Caug = [sbuf(f"Caug{p}", (128, 66), F32) for p in range(1)]
    Ct = [sbuf(f"Ct{p}", (128, E), F32) for p in range(1)]
    l2t = [sbuf(f"l2t{p}", (128, E), F32) for p in range(1)]
    mk1 = [sbuf(f"mk1{p}", (128, E), F32) for p in range(1)]
    mk2 = [sbuf(f"mk2{p}", (128, E), F32) for p in range(1)]
    cmb = [sbuf(f"cmb{p}", (128, E), F32) for p in range(1)]
    cm2 = [sbuf(f"cm2{p}", (128, E), F32) for p in range(TB // 128)]
    m1t = [sbuf(f"m1t{p}", (128, 1), F32) for p in range(1)]
    m2t = [sbuf(f"m2t{p}", (128, 1), F32) for p in range(1)]
    dmt = [sbuf(f"dmt{p}", (128, 1), F32) for p in range(1)]
    w1t = [sbuf(f"w1t{p}", (128, 1), F32) for p in range(1)]
    w2t = [sbuf(f"w2t{p}", (128, 1), F32) for p in range(1)]
    bq_sb = sbuf("bqs", (128, KC), F32)
    bk_sb = sbuf("bks", (128, KC), F32)
    bv_sb = sbuf("bvs", (128, KC), F32)
    bg_sb = sbuf("bgs", (128, KC), F32)
    bsg_sb = sbuf("bsgs", (128, MS), F32)
    bsu_sb = sbuf("bsus", (128, MS), F32)
    bsd_sb = sbuf("bsds", (128, KC), F32)
    bsdh_sb = sbuf("bsdh", (128, KC), F32)
    rwt_sb = sbuf("rwts", (128, KC * E), BF16)
    iden_sb = sbuf("idens", (128, 128), F32)
    oneh_sb = sbuf("onehs", (E, E * 128), BF16)
    onesc = sbuf("onesc", (128, TB), BF16)
    csr_sb = sbuf("csr", (1, E), F32)
    ncsb_sb = sbuf("ncsb", (128, E), F32)
    onebf = sbuf("onebf", (128, 1), BF16)
    onef1 = sbuf("onef1", (1, 128), F32)
    carry = sbuf("carry", (128, KC), F32)

    NBANK = 8 if NW <= 512 else 4
    PSB = [psum(f"PS{i}", (128, NW), F32) for i in range(NBANK)]

    P = Plan()
    _pring = [0]

    def pgrab():
        i = _pring[0] % NBANK
        _pring[0] += 1
        return PSB[i], f"PS{i}"

    def pgrabs(n):
        return [pgrab() for _ in range(n)]

    def setup():
        for nm, t in (("bq", bq_sb), ("bk", bk_sb), ("bv", bv_sb),
                      ("bg", bg_sb), ("bsg", bsg_sb), ("bsu", bsu_sb),
                      ("bsd", bsd_sb), ("iden", iden_sb), ("oneh", oneh_sb)):
            src = dr[nm].ap()
            P.emit("qs", lambda E_, o=t, i=src: E_.dma_start(o[:], i),
                   writes=(t.name,))
        P.emit("qg", lambda E_: E_.dma_start(
            rwt_sb[:], dr["rwt"].ap().rearrange("p k e -> p (k e)")),
            writes=("rwts",))
        P.emit("dve", lambda E_: E_.memset(onesc[:], 1.0), writes=("onesc",))
        P.emit("dve", lambda E_: E_.memset(lgaug[:], 0.0), writes=("lgaug",))
        P.emit("dve", lambda E_: E_.memset(onebf[:], 1.0), writes=("onebf",))
        P.emit("dve", lambda E_: E_.memset(onef1[:], 1.0), writes=("onef1",))
        P.emit("dve", lambda E_: E_.memset(carry[:], 0.0),
               writes=tuple(f"carry{k}" for k in range(KC)))
        P.emit("dve", lambda E_: E_.tensor_scalar_mul(bsdh_sb[:], bsd_sb[:], 0.5),
               reads=("bsds",), writes=("bsdh",))
        cs_t, cs_n = pgrab()
        for k in range(KC):
            P.emit("pe", lambda E_, k=k, cs_t=cs_t: E_.matmul(
                cs_t[0:1, 0:E], onebf[:], rwt_sb[:, k * E:(k + 1) * E],
                start=(k == 0), stop=(k == KC - 1)),
                reads=("rwts", "onebf"), writes=(cs_n,), inc=(k == KC - 1))
        P.emit("dve", lambda E_, cs_t=cs_t: E_.tensor_scalar_mul(
            csr_sb[:], cs_t[0:1, 0:E], -1.0),
            reads=(cs_n,), writes=("csr",))
        nb_t, nb_n = pgrab()
        P.emit("pe", lambda E_, nb_t=nb_t: E_.matmul(
            nb_t[:, 0:E], onef1[:], csr_sb[:], start=True, stop=True),
            reads=("csr", "onef1"), writes=(nb_n,))
        P.emit("dve", lambda E_, nb_t=nb_t: E_.tensor_copy(ncsb_sb[:], nb_t[:, 0:E]),
               reads=(nb_n,), writes=("ncsb",))

    def emit_ln(with_router):
        sxa = pgrabs(NH)
        sxb = pgrabs(NH)
        raw = pgrabs(NH) if with_router else None
        for k in range(KC):
            p = k % 2
            P.emit("dve", lambda E_, k=k, p=p: E_.tensor_copy(xc_sb[p][:], x_sb[k][:]),
                   reads=(f"x{k}",), writes=(f"xc{p}",))
            P.emit("dve", lambda E_, p=p: E_.tensor_mul(xq_sb[0][:], xc_sb[p][:], xc_sb[p][:]),
                   reads=(f"xc{p}",), writes=("xq0",))
            for h, (a, b) in enumerate(HAL):
                P.emit("pe", lambda E_, p=p, a=a, b=b, k=k, t=sxa[h][0]: E_.matmul(
                    t[0:1, :], onebf[:], xc_sb[p][:, a:b],
                    start=(k == 0), stop=(k == KC - 1)),
                    reads=(f"xc{p}", "onebf"), writes=(sxa[h][1],), inc=False)
                if with_router:
                    P.emit("pe", lambda E_, p=p, a=a, b=b, k=k, t=raw[h][0]: E_.matmul(
                        t[0:E, :], rwt_sb[:, k * E:(k + 1) * E], xc_sb[p][:, a:b],
                        start=(k == 0), stop=(k == KC - 1)),
                        reads=(f"xc{p}", "rwts"), writes=(raw[h][1],), inc=False)
                P.emit("pe", lambda E_, p=p, a=a, b=b, k=k, t=sxb[h][0]: E_.matmul(
                    t[0:1, :], onebf[:], xq_sb[0][:, a:b],
                    start=(k == 0), stop=(k == KC - 1)),
                    reads=("xq0", "onebf"), writes=(sxb[h][1],),
                    inc=(h == NH - 1))
        for h, (a, b) in enumerate(HAL):
            P.emit("act", lambda E_, a=a, b=b, t=sxa[h][0]: E_.activation(
                mean_r[:, a:b], t[0:1, :], AF.Copy, scale=1.0 / D),
                reads=(sxa[h][1],), writes=("meanr",))
            P.emit("dve", lambda E_, a=a, b=b, t=sxb[h][0]: E_.tensor_scalar_mul(
                r2_r[:, a:b], t[0:1, :], 1.0 / D),
                reads=(sxb[h][1],), writes=("r2r",))
        P.emit("dve", lambda E_: E_.tensor_mul(r3_r[:], mean_r[:], mean_r[:]),
               reads=("meanr",), writes=("r3r",))
        P.emit("dve", lambda E_: E_.tensor_sub(r2_r[:], r2_r[:], r3_r[:]),
               reads=("r2r", "r3r"), writes=("r2r",))
        P.emit("dve", lambda E_: E_.tensor_scalar_add(r2_r[:], r2_r[:], EPS),
               reads=("r2r",), writes=("r2r",))
        P.emit("act", lambda E_: E_.activation(r3_r[:], r2_r[:], AF.Ln),
               reads=("r2r",), writes=("r3r",))
        P.emit("act", lambda E_: E_.activation(rstd_r[:], r3_r[:], AF.Exp, scale=-0.5),
               reads=("r3r",), writes=("rstdr",))
        mb = pgrabs(NH)
        rb = pgrabs(NH)
        for h, (a, b) in enumerate(HAL):
            P.emit("pe", lambda E_, a=a, b=b, t=mb[h][0]: E_.matmul(
                t[:], onef1[:], mean_r[:, a:b], start=True, stop=True),
                reads=("meanr", "onef1"), writes=(mb[h][1],))
            P.emit("pe", lambda E_, a=a, b=b, t=rb[h][0]: E_.matmul(
                t[:], onef1[:], rstd_r[:, a:b], start=True, stop=True),
                reads=("rstdr", "onef1"), writes=(rb[h][1],))
        if with_router:
            for h, (a, b) in enumerate(HAL):
                P.emit("dve", lambda E_, a=a, b=b, t=raw[h][0]: E_.tensor_copy(
                    lgaug[0:E, a:b], t[0:E, :]),
                    reads=(raw[h][1],), writes=("lgaug",))
            P.emit("dve", lambda E_: E_.tensor_copy(lgaug[32:33, :], mean_r[:]),
                   reads=("meanr",), writes=("lgaug",))
            P.emit("dve", lambda E_: E_.tensor_copy(lgaug[64:65, :], rstd_r[:]),
                   reads=("rstdr",), writes=("lgaug",))
        for k in range(KC):
            for h, (a, b) in enumerate(HAL):
                P.emit("dve", lambda E_, k=k, a=a, b=b, t=mb[h][0]: E_.tensor_sub(
                    dtmp[:, a:b], x_sb[k][:, a:b], t[:]),
                    reads=(f"x{k}", mb[h][1]), writes=("dtmp",))
                P.emit("dve", lambda E_, k=k, a=a, b=b, t=rb[h][0]: E_.tensor_mul(
                    xn_sb[k][:, a:b], dtmp[:, a:b], t[:]),
                    reads=("dtmp", rb[h][1]), writes=(f"xn{k}",))
        return raw

    def router_group_quanta(raw):
        for g in range(NG):
            def quantum(g=g):
                p = 0
                c0 = g * 128
                tr_t, tr_n = pgrab()
                P.emit("pe", lambda E_, c0=c0, t=tr_t: E_.transpose(
                    t[:, 0:66], lgaug[0:66, c0:c0 + 128], iden_sb[0:66, 0:66]),
                    reads=("lgaug", "idens"), writes=(tr_n,))
                P.emit("dve", lambda E_, p=p, t=tr_t: E_.tensor_copy(Caug[p][:], t[:, 0:66]),
                       reads=(tr_n,), writes=(f"Caug{p}",))
                P.emit("dve", lambda E_, p=p: E_.scalar_tensor_tensor(
                    Ct[p][:], ncsb_sb[:], Caug[p][:, 32:33], Caug[p][:, 0:E],
                    ALU.mult, ALU.add),
                    reads=("ncsb", f"Caug{p}"), writes=(f"Ct{p}",))
                P.emit("dve", lambda E_, p=p: E_.tensor_single_scalar(
                    Ct[p][:], Ct[p][:], Caug[p][:, 64:65], ALU.mult),
                    reads=(f"Ct{p}", f"Caug{p}"), writes=(f"Ct{p}",))
                P.emit("dve", lambda E_, p=p: E_.reduce_max(m1t[p][:], Ct[p][:], AX.X),
                       reads=(f"Ct{p}",), writes=(f"m1t{p}",))
                P.emit("dve", lambda E_, p=p: E_.tensor_single_scalar(
                    mk1[p][:], Ct[p][:], m1t[p][:], ALU.is_equal),
                    reads=(f"Ct{p}", f"m1t{p}"), writes=(f"mk1{p}",))
                P.emit("dve", lambda E_, p=p: E_.scalar_tensor_tensor(
                    l2t[p][:], mk1[p][:], NEG, Ct[p][:], ALU.mult, ALU.add),
                    reads=(f"mk1{p}", f"Ct{p}"), writes=(f"l2t{p}",))
                P.emit("dve", lambda E_, p=p: E_.reduce_max(m2t[p][:], l2t[p][:], AX.X),
                       reads=(f"l2t{p}",), writes=(f"m2t{p}",))
                P.emit("dve", lambda E_, p=p: E_.tensor_sub(dmt[p][:], m1t[p][:], m2t[p][:]),
                       reads=(f"m1t{p}", f"m2t{p}"), writes=(f"dmt{p}",))
                P.emit("act", lambda E_, p=p: E_.activation(w1t[p][:], dmt[p][:], AF.Sigmoid),
                       reads=(f"dmt{p}",), writes=(f"w1t{p}",))
                P.emit("act", lambda E_, p=p: E_.activation(w2t[p][:], dmt[p][:], AF.Sigmoid, scale=-1.0),
                       reads=(f"dmt{p}",), writes=(f"w2t{p}",))
                P.emit("dve", lambda E_, p=p: E_.tensor_single_scalar(
                    mk2[p][:], l2t[p][:], m2t[p][:], ALU.is_equal),
                    reads=(f"l2t{p}", f"m2t{p}"), writes=(f"mk2{p}",))
                P.emit("dve", lambda E_, p=p: E_.tensor_single_scalar(
                    cmb[p][:], mk1[p][:], w1t[p][:], ALU.mult),
                    reads=(f"mk1{p}", f"w1t{p}"), writes=(f"cmb{p}",))
                P.emit("dve", lambda E_, p=p, g=g: E_.scalar_tensor_tensor(
                    cm2[g][:], mk2[p][:], w2t[p][:], cmb[p][:], ALU.mult, ALU.add),
                    reads=(f"mk2{p}", f"w2t{p}", f"cmb{p}"), writes=(f"cm2{g}",))

            yield quantum
        for g in range(NG):
            def quantum2(g=g):
                c0 = g * 128
                tc_t, tc_n = pgrab()
                P.emit("pe", lambda E_, g=g, t=tc_t: E_.transpose(
                    t[0:E, 0:128], cm2[g][:], iden_sb[:]),
                    reads=(f"cm2{g}", "idens"), writes=(tc_n,))
                P.emit("dve", lambda E_, c0=c0, t=tc_t: E_.tensor_copy(
                    TC_sb[:, c0:c0 + 128], t[0:E, 0:128]),
                    reads=(tc_n,), writes=("TC",))

            yield quantum2
    wpar = dict(g=0, u=0, d=0, a=0)
    seen_w = set()
    rr = [0]

    def load_w(stream, buf_list, kchunks, base, skey, apf):
        p = wpar[stream] % len(buf_list)
        wpar[stream] += 1
        buf = buf_list[p]
        nm = buf.name
        n = kchunks * 128
        if skey not in seen_w:
            seen_w.add(skey)
            ap = apf(dr[base]).rearrange("p k m -> p (k m)")
            P.emit("qg", lambda E_, o=buf, i=ap, n=n:
                   E_.dma_start(o[:, 0:n], i), writes=(nm,))
            sap = apf(drb[base]).rearrange("p k m -> p (k m)")
            P.emit("qs", lambda E_, o=sap, i=buf, n=n:
                   E_.dma_start(o, i[:, 0:n]),
                   reads=(nm,), writes=("scr:" + skey,))
        else:
            q = ("qs", "qg")[rr[0] % 2]
            rr[0] += 1
            sap = apf(drb[base]).rearrange("p k m -> p (k m)")
            P.emit(q, lambda E_, o=buf, i=sap, n=n:
                   E_.dma_start(o[:, 0:n], i),
                   reads=("scr:" + skey,), writes=(nm,))
        return buf, nm

    def mm_block(pt, pn, a, b, wbuf, wnm, rhs_tiles, rhs_names):
        n = len(rhs_tiles)
        for k in range(n):
            P.emit("pe", lambda E_, k=k: E_.matmul(
                pt[:], wbuf[:, k * 128:(k + 1) * 128], rhs_tiles[k][:, a:b],
                start=(k == 0), stop=(k == n - 1)),
                reads=(wnm, rhs_names[k]),
                writes=(pn,), inc=(k == n - 1))

    def emit_bc(e):
        bcb = pgrabs(NH)
        for h, (a, b) in enumerate(HAL):
            P.emit("pe", lambda E_, a=a, b=b, e=e, t=bcb[h][0]: E_.matmul(
                t[:], oneh_sb[:, e * 128:(e + 1) * 128], TC_sb[:, a:b],
                start=True, stop=True),
                reads=("TC", "onehs"), writes=(bcb[h][1],))
            P.emit("dve", lambda E_, a=a, b=b, t=bcb[h][0]: E_.tensor_copy(
                bc_sb[:, a:b], t[:]),
                reads=(bcb[h][1],), writes=("bct",))

    def emit_expert_gu(e, rq=None, use_bc=False):
        xn_names = [f"xn{k}" for k in range(KC)]
        for m in range(MI):
            if rq is not None:
                q = next(rq, None)
                if q is not None:
                    q()
            gb, gnm = load_w("g", WG, KC, "gw", f"gw{e}_{m}",
                             lambda h, e=e, m=m: h.ap()[e, m])
            ub, unm = load_w("u", WU, KC, "uw", f"uw{e}_{m}",
                             lambda h, e=e, m=m: h.ap()[e, m])
            Pg = pgrabs(NH)
            Pu = pgrabs(NH)
            for h, (a, b) in enumerate(HAL):
                mm_block(Pg[h][0], Pg[h][1], a, b, gb, gnm, xn_sb, xn_names)
                mm_block(Pu[h][0], Pu[h][1], a, b, ub, unm, xn_sb, xn_names)
                P.emit("act", lambda E_, a=a, b=b, t=Pg[h][0]: E_.activation(
                    sg_sb[:, a:b], t[:], AF.Silu),
                    reads=(Pg[h][1],), writes=(f"sg{h}",))
                if use_bc:
                    P.emit("dve", lambda E_, a=a, b=b, t=Pu[h][0]: E_.tensor_mul(
                        tu_sb[:, a:b], t[:], bc_sb[:, a:b]),
                        reads=(Pu[h][1], "bct"), writes=(f"tu{h}",))
                else:
                    P.emit("dve", lambda E_, a=a, b=b, t=Pu[h][0]: E_.tensor_copy(
                        tu_sb[:, a:b], t[:]),
                        reads=(Pu[h][1],), writes=(f"tu{h}",))
                P.emit("dve", lambda E_, a=a, b=b, m=m: E_.tensor_mul(
                    hp_sb[m][:, a:b], sg_sb[:, a:b], tu_sb[:, a:b]),
                    reads=(f"sg{h}", f"tu{h}"), writes=(f"hp{m}_{h}",))

    def emit_expert_down(e, scale, use_bc=False):
        if not use_bc:
            emit_bc(e)
        for j in range(KC):
            db, dnm = load_w("d", WD, MI, "dwn", f"dw{e}_{j}",
                             lambda h, e=e, j=j: h.ap()[e, j])
            Pd = pgrabs(NH)
            for h, (a, b) in enumerate(HAL):
                mm_block(Pd[h][0], Pd[h][1], a, b, db, dnm, hp_sb,
                         [f"hp{k}_{h}" for k in range(MI)])
                if use_bc:
                    P.emit("dve", lambda E_, a=a, b=b, j=j, t=Pd[h][0], s=scale:
                           E_.scalar_tensor_tensor(
                               x_sb[j][:, a:b], t[:], s,
                               x_sb[j][:, a:b], ALU.mult, ALU.add),
                           reads=(Pd[h][1], f"x{j}"), writes=(f"x{j}",))
                else:
                    P.emit("dve", lambda E_, a=a, b=b, t=Pd[h][0]: E_.tensor_mul(
                        dtmp[:, a:b], t[:], bc_sb[:, a:b]),
                        reads=(Pd[h][1], "bct"), writes=("dtmp",))
                    P.emit("dve", lambda E_, a=a, b=b, j=j, s=scale:
                           E_.scalar_tensor_tensor(
                               x_sb[j][:, a:b], dtmp[:, a:b], s, x_sb[j][:, a:b],
                               ALU.mult, ALU.add),
                           reads=("dtmp", f"x{j}"), writes=(f"x{j}",))

    def emit_shared(scale):
        xn_names = [f"xn{k}" for k in range(KC)]
        for gi, grp in enumerate(SGRP):
            for mi, m in enumerate(grp):
                gb, gnm = load_w("g", WG, KC, "sgw", f"sg{m}",
                                 lambda h, m=m: h.ap()[m])
                ub, unm = load_w("u", WU, KC, "suw", f"su{m}",
                                 lambda h, m=m: h.ap()[m])
                Pg = pgrabs(NH)
                Pu = pgrabs(NH)
                for h, (a, b) in enumerate(HAL):
                    mm_block(Pg[h][0], Pg[h][1], a, b, gb, gnm, xn_sb, xn_names)
                    mm_block(Pu[h][0], Pu[h][1], a, b, ub, unm, xn_sb, xn_names)
                    P.emit("act", lambda E_, a=a, b=b, t=Pg[h][0], m=m: E_.activation(
                        sg_sb[:, a:b], t[:], AF.Silu,
                        bias=bsg_sb[:, m:m + 1]),
                        reads=(Pg[h][1], "bsgs"), writes=(f"sg{h}",))
                    P.emit("dve", lambda E_, a=a, b=b, t=Pu[h][0], m=m:
                           E_.tensor_single_scalar(
                               tu_sb[:, a:b], t[:], bsu_sb[:, m:m + 1],
                               ALU.add),
                           reads=(Pu[h][1], "bsus"), writes=(f"tu{h}",))
                    P.emit("dve", lambda E_, a=a, b=b, mi=mi: E_.tensor_mul(
                        hp_sb[mi][:, a:b], sg_sb[:, a:b], tu_sb[:, a:b]),
                        reads=(f"sg{h}", f"tu{h}"), writes=(f"hp{mi}_{h}",))
            lastg = gi == len(SGRP) - 1
            for j in range(KC):
                db, dnm = load_w(
                    "d", WD, len(grp), "sdw", f"sd{j}_{gi}",
                    lambda h, j=j, g0=grp[0], gl=len(grp):
                        h.ap()[j][:, g0:g0 + gl, :])
                Pd = pgrabs(NH)
                for h, (a, b) in enumerate(HAL):
                    mm_block(Pd[h][0], Pd[h][1], a, b, db, dnm, hp_sb,
                             [f"hp{k}_{h}" for k in range(len(grp))])
                    if lastg:
                        P.emit("dve", lambda E_, a=a, b=b, j=j, t=Pd[h][0], s=scale:
                               E_.tensor_scalar(dtmp[:, a:b], t[:], s,
                                                bsdh_sb[:, j:j + 1],
                                                ALU.mult, ALU.add),
                               reads=(Pd[h][1], "bsdh"), writes=("dtmp",))
                        P.emit("dve", lambda E_, a=a, b=b, j=j: E_.tensor_add(
                            x_sb[j][:, a:b], x_sb[j][:, a:b], dtmp[:, a:b]),
                            reads=("dtmp", f"x{j}"), writes=(f"x{j}",))
                    else:
                        P.emit("dve", lambda E_, a=a, b=b, j=j, t=Pd[h][0], s=scale:
                               E_.scalar_tensor_tensor(
                                   x_sb[j][:, a:b], t[:], s,
                                   x_sb[j][:, a:b], ALU.mult, ALU.add),
                               reads=(Pd[h][1], f"x{j}"), writes=(f"x{j}",))

    def emit_attn():
        xn_names = [f"xn{k}" for k in range(KC)]
        for ec in range(KC):
            pro = {}
            for nm in ("waq", "wak", "wav", "wag"):
                wb, wnm = load_w("a", WA, KC, nm, f"{nm}{ec}",
                                 lambda h, ec=ec: h.ap()[ec])
                pro[nm] = pgrabs(NH)
                for h, (a, b) in enumerate(HAL):
                    mm_block(pro[nm][h][0], pro[nm][h][1], a, b, wb, wnm,
                             xn_sb, xn_names)
            for h, (a, b) in enumerate(HAL):
                P.emit("act", lambda E_, a=a, b=b, ec=ec, t=pro["wag"][h][0]:
                       E_.activation(ag[:, a:b], t[:], AF.Sigmoid,
                                     bias=bg_sb[:, ec:ec + 1]),
                       reads=(pro["wag"][h][1], "bgs"), writes=("bct",))
                P.emit("dve", lambda E_, a=a, b=b, ec=ec, t=pro["wav"][h][0]:
                       E_.tensor_single_scalar(
                           av[:, a:b], t[:], bv_sb[:, ec:ec + 1], ALU.add),
                       reads=(pro["wav"][h][1], "bvs"), writes=("av",))
                P.emit("dve", lambda E_, a=a, b=b, ec=ec, t=pro["wak"][h][0]:
                       E_.scalar_tensor_tensor(
                           av[:, a:b], t[:], bk_sb[:, ec:ec + 1], av[:, a:b],
                           ALU.add, ALU.mult),
                       reads=(pro["wak"][h][1], "bks", "av"), writes=("av",))
                P.emit("dve", lambda E_, a=a, b=b, ec=ec, t=pro["waq"][h][0]:
                       E_.scalar_tensor_tensor(
                           aq[:, a:b], t[:], bq_sb[:, ec:ec + 1], ag[:, a:b],
                           ALU.add, ALU.mult),
                       reads=(pro["waq"][h][1], "bqs", "bct"), writes=("dtmp",))
            P.emit("dve", lambda E_, ec=ec: E_.tensor_tensor_scan(
                aS[:], onesc[:], av[:], carry[:, ec:ec + 1], ALU.mult, ALU.add),
                reads=("onesc", "av", f"carry{ec}"), writes=("aS",))
            P.emit("dve", lambda E_, ec=ec: E_.tensor_copy(
                carry[:, ec:ec + 1], aS[:, TB - 1:TB]),
                reads=("aS",), writes=(f"carry{ec}",))
            P.emit("dve", lambda E_: E_.tensor_mul(ag[:], aq[:], aS[:]),
                   reads=("dtmp", "aS"), writes=("bct",))
            P.emit("dve", lambda E_, ec=ec: E_.tensor_add(
                x_sb[ec][:], x_sb[ec][:], ag[:]),
                reads=("bct", f"x{ec}"), writes=(f"x{ec}",))

    setup()
    for c in range(NCH):
        c0 = c * TB
        for k in range(KC):
            src = dr["xT"].ap()[k * 128:(k + 1) * 128, c0:c0 + TB]
            P.emit("qs", lambda E_, k=k, src=src: E_.dma_start(x_sb[k][:], src),
                   writes=(f"x{k}",))
        for app in range(2):
            raw = emit_ln(with_router=True)
            rq = router_group_quanta(raw)
            emit_expert_gu(0, rq)
            for q in rq:
                q()
            emit_expert_down(0, 0.5)
            for e in range(1, E):
                emit_bc(e)
                emit_expert_gu(e, use_bc=True)
                emit_expert_down(e, 0.5, use_bc=True)
            emit_shared(0.5)
            if app == 0:
                emit_ln(with_router=False)
                emit_attn()
        for k in range(KC):
            dst = dr["yT"].ap()[k * 128:(k + 1) * 128, c0:c0 + TB]
            P.emit("qs", lambda E_, k=k, dst=dst: E_.dma_start(dst, x_sb[k][:]),
                   reads=(f"x{k}",))

    sems = {n: st.enter_context(nc.semaphore(f"sem_{n}")) for n in P.sem_names}
    blk = st.enter_context(nc.Block())

    def body(items, final_waits=()):
        def run(E_):
            for it in items:
                if it[0] == "w":
                    E_.wait_ge(sems[it[1]], it[2])
                else:
                    inst = it[1](E_)
                    if it[2] is not None:
                        inst.then_inc(sems[it[2]], 16 if it[2][0] == "q" else 1)
            for s, v in final_waits:
                E_.wait_ge(sems[s], v)
        return run

    qs_final = [(s, P.cnt[s]) for s in P.sem_names
                if s.startswith("qs") and P.cnt[s] > 0]
    blk.tensor(body(P.ops["pe"]))
    blk.vector(body(P.ops["dve"]))
    blk.scalar(body(P.ops["act"]))
    blk.sync(body(P.ops["qs"], qs_final))
    blk.gpsimd(body(P.ops["qg"]))
    st.close()

    n_inst = {k: len([i for i in v if i[0] == "i"]) for k, v in P.ops.items()}
    n_wait = {k: len([i for i in v if i[0] == "w"]) for k, v in P.ops.items()}
    info = dict(sbuf_mb=sb_bytes[0] / 2**20, n_inst=n_inst, n_wait=n_wait)
    return nc, info


def blockA(W, KCd, MId):
    return np.ascontiguousarray(
        np.asarray(W, dtype=np.float32).reshape(KCd, 128, MId, 128).transpose(2, 1, 0, 3))


def bias_cols(b, n):
    return np.ascontiguousarray(np.asarray(b, dtype=np.float32).reshape(n, 128).T)


def prep_weights(inp, cfg, pfx="L0"):
    D, I, SI, E = cfg["D"], cfg["I"], cfg["SI"], cfg["E"]
    KC, MI, MS = D // 128, I // 128, SI // 128
    g = lambda k: np.asarray(inp[f"{pfx}_{k}"], dtype=np.float32)
    w = {}
    w["gw"] = np.stack([blockA(g("ffn_gate_w")[e], KC, MI) for e in range(E)])
    w["uw"] = np.stack([blockA(g("ffn_up_w")[e], KC, MI) for e in range(E)])
    w["dwn"] = np.stack([blockA(g("ffn_down_w")[e], MI, KC) for e in range(E)])
    w["sgw"] = blockA(g("ffn_s_gate_w"), KC, MS)
    w["suw"] = blockA(g("ffn_s_up_w"), KC, MS)
    w["sdw"] = blockA(g("ffn_s_down_w"), MS, KC)
    w["waq"] = blockA(g("attn_Wq").T, KC, KC)
    w["wak"] = blockA(g("attn_Wk").T, KC, KC)
    w["wav"] = blockA(g("attn_Wv").T, KC, KC)
    w["wag"] = blockA(g("attn_Wg").T, KC, KC)
    w["rwt"] = np.ascontiguousarray(
        g("ffn_router_w").T.reshape(KC, 128, E).transpose(1, 0, 2))
    w["bq"] = bias_cols(g("attn_bq"), KC)
    w["bk"] = bias_cols(g("attn_bk"), KC)
    w["bv"] = bias_cols(g("attn_bv"), KC)
    w["bg"] = bias_cols(g("attn_bg"), KC)
    w["bsg"] = bias_cols(g("ffn_s_gate_b"), MS)
    w["bsu"] = bias_cols(g("ffn_s_up_b"), MS)
    w["bsd"] = bias_cols(g("ffn_s_down_b"), KC)
    w["iden"] = np.eye(128, dtype=np.float32)
    import ml_dtypes
    oneh = np.zeros((E, E * 128), dtype=ml_dtypes.bfloat16)
    for e in range(E):
        oneh[e, e * 128:(e + 1) * 128] = 1.0
    w["oneh"] = oneh
    return w


_CACHE = {}


def kernel(**inputs):
    from concourse.bass_utils import run_bass_kernel_spmd
    cfg = CFG_REAL
    if "nc" not in _CACHE:
        _CACHE["nc"], _CACHE["info"] = build_program(cfg)
    nc = _CACHE["nc"]
    w = prep_weights(inputs, cfg)
    x = np.asarray(inputs["L0_attn_x"], dtype=np.float32)
    n = x.shape[0]
    in_maps = []
    for c in range(n):
        m = dict(w)
        m["xT"] = np.ascontiguousarray(x[c].T)
        in_maps.append(m)
    res = run_bass_kernel_spmd(nc, in_maps, core_ids=list(range(n)))
    out = np.stack([np.asarray(r["yT"], dtype=np.float32).T for r in res.results])
    return np.ascontiguousarray(out)
